# version 04
# baseline (speedup 1.0000x reference)
import os
import numpy as np
import ml_dtypes
from contextlib import ExitStack

import concourse.bass as bass
import concourse.mybir as mybir
from concourse.bass_utils import run_bass_kernel_spmd

F32 = mybir.dt.float32
BF16 = mybir.dt.bfloat16
AF = mybir.ActivationFunctionType
ALU = mybir.AluOpType
AX = mybir.AxisListType
BF = ml_dtypes.bfloat16

DM = 2048
DI = 4096
NH = 32
DH = 128
DS = 64
CONV_DIM = 4224
DPROJ = 8352
CHUNK = 256
NCK = 16
T = 4096
CTX = 128
TP = T + CTX
HS = NH * CHUNK
NEG = -1.0e30
EPS = 1e-5

TGS = [(i * 512, 512) for i in range(8)] + [(4096, 128)]
EGROUPS = [list(range(g * 6, min(g * 6 + 6, 65))) for g in range(11)]
NGRP = len(EGROUPS)
PHASES = ["Q", "A1", "A2", "X", "B1", "B2", "B3"]
A1_PARAMS_ON_SYNC = True


def hP(h):
    return 64 * (h // 16), 128 * (h % 16)


def hCS(h):
    return 64 * (h // 16), 256 * (h % 16)


A1_NO_MM = False
A1_MM_K = 16
A1_NO_SILU = False
A1_NO_ULOAD = False
A1_LIMIT = None
A1_NO_WTDT = False
A1_NO_SPILL = False


def host_consts():
    c = {}
    c["ident128"] = np.eye(128, dtype=BF)
    ind = np.zeros((64, HS), dtype=np.float32)
    for h in range(NH):
        ind[h, h * CHUNK:(h + 1) * CHUNK] = 1.0
        ind[32 + h, h * CHUNK:(h + 1) * CHUNK] = 1.0
    c["indicator"] = ind.astype(BF)
    u1 = np.zeros((128, 128), np.float32)
    v1 = np.zeros((128, 256), np.float32)
    for j in range(127):
        u1[j, j + 1:] = 1.0
        v1[j, j] = NEG
    c["maskU1"] = u1.astype(BF)
    c["maskV1w"] = np.concatenate([v1, v1], axis=1).astype(BF)
    u2 = np.zeros((128, 128), np.float32)
    v2 = np.zeros((128, 256), np.float32)
    u2[0, :] = 1.0
    v2[0, 0:128] = NEG
    for j in range(127):
        u2[1 + j, j + 1:] = 1.0
        v2[1 + j, 128 + j] = NEG
    c["maskU2"] = u2.astype(BF)
    c["maskV2w"] = np.concatenate([v2, v2], axis=1).astype(BF)
    c["negones2"] = np.full((2, 256), -1.0, dtype=BF)
    c["ones32"] = np.ones((32, 256), np.float32)
    c["onesb_1x64"] = np.ones((1, 64), dtype=BF)
    c["onesb_128x1"] = np.ones((128, 1), dtype=BF)
    c["onesf_1x128"] = np.ones((1, 128), np.float32)
    c["onesb_1x128"] = np.ones((1, 128), dtype=BF)
    c["onesf_1x64"] = np.ones((1, 64), np.float32)
    return c


def percore_consts(core):
    b, s = core // 4, core % 4
    A = np.zeros((8, 8), np.float32)
    bias = np.full((8, 1), NEG, np.float32)
    for j in range(8):
        jb, js = j // 4, j % 4
        if jb == b and js < s:
            bias[j, 0] = 0.0
            for k in range(8):
                kb, ks = k // 4, k % 4
                if kb == b and js < ks < s:
                    A[j, k] = 1.0
    return {"combA": np.ascontiguousarray(A.T), "combB": bias}


class Ctx:
    def __init__(self, nc, stack):
        self.nc = nc
        self.stack = stack
        self.sems = {}
        self.cnt = {}

    def sem(self, name):
        if name not in self.sems:
            self.sems[name] = self.stack.enter_context(self.nc.semaphore(name))
            self.cnt[name] = 0
        return self.sems[name]

    def snap(self):
        for n in ("LD", "LS", "LG", "MM", "EV", "ES", "XSs", "XSv", "CC"):
            self.sem(n)
        return dict(self.cnt)


SCRATCH = {
    "u_bf":    ([TP, DM], BF16, 0),
    "w_in_bf": ([DPROJ, DM], BF16, 0),
    "wo_sp":   ([DM, DI], BF16, 0),
    "q_sp":    ([DM, DM], BF16, 0),
    "x_sp":    ([DI, TP], BF16, 1),
    "g_sp":    ([DI, TP], BF16, 1),
    "b_sp":    ([DS, TP], BF16, 1),
    "c_sp":    ([DS, TP], BF16, 1),
    "dt_sp":   ([NH, TP], F32, 1),
    "dttm_sp": ([128, 33 * NH], F32, 1),
    "nhl_sp":  ([NCK, 2, HS], BF16, 2),
    "expc_sp": ([NCK, HS], BF16, 2),
    "cl_sp":   ([NCK, NH], F32, 2),
    "cdp_sp":  ([NCK, NH], F32, 3),
    "cs_sp":   ([NCK, 128, 2 * DM], BF16, 2),
    "y_sp":    ([NCK, DH, HS], BF16, 2),
    "exch_in": ([130, DM], F32, 2),
    "exch_out": ([8 * 130, DM], F32, 3),
    "yn_sp":   ([NCK, DH, HS], BF16, 4),
    "dbg_sp":  ([8, 128, 512], F32, 3),
    "x1f_sp":  ([T, DM], F32, 5),
    "x1b_sp":  ([T, DM], BF16, 5),
}


def build_nc(lo=0, hi=6, expose=()):
    nc = bass.Bass("TRN2", target_bir_lowering=False)
    inp = {}

    def ein(name, shape, dtype):
        inp[name] = nc.dram_tensor(name, shape, dtype, kind="ExternalInput").ap()

    ein("u_pad", [TP, DM], F32)
    ein("w_in", [DPROJ, DM], F32)
    ein("conv_w", [CONV_DIM, 4], F32)
    ein("conv_b", [CONV_DIM, 1], F32)
    ein("dt_bias", [NH, 1], F32)
    ein("a_log", [NH, 1], F32)
    ein("d_vec", [1, NH], F32)
    ein("norm_w", [1, DI], F32)
    ein("w_out", [DM, DI], F32)
    ein("ffn_w", [DM, DM], F32)
    ein("ffn_b", [1, DM], F32)
    for k, v in host_consts().items():
        ein(k, list(v.shape), BF16 if v.dtype == BF else F32)
    ein("combA", [8, 8], F32)
    ein("combB", [8, 1], F32)

    dr = {}
    dump = {}
    for name, (shape, dtype, prod) in SCRATCH.items():
        kind = "ExternalInput" if prod < lo else "Internal"
        dr[name] = nc.dram_tensor(name, shape, dtype, kind=kind).ap()
        if name in expose:
            dump[name] = nc.dram_tensor("dump_" + name, shape, dtype,
                                        kind="ExternalOutput").ap()

    out_ap = nc.dram_tensor("out", [T, DM], F32,
                            kind="ExternalOutput" if hi >= 6 else "Internal").ap()

    with ExitStack() as stack:
        ctx = Ctx(nc, stack)
        ctx.inp = inp
        ctx.d = dr
        ctx.out_ap = out_ap
        block = stack.enter_context(nc.Block())
        ctx.block = block

        P = {}
        def persist(name, shape, dtype):
            P[name] = stack.enter_context(nc.sbuf_tensor("p_" + name, shape, dtype))
        persist("ident128", [128, 128], BF16)
        persist("onesb_1x64", [1, 64], BF16)
        persist("onesb_1x128", [1, 128], BF16)
        persist("onesb_128x1", [128, 1], BF16)
        persist("onesf_1x128", [1, 128], F32)
        persist("onesf_1x64", [1, 64], F32)
        persist("ones32", [32, 256], F32)
        persist("a_col", [32, 1], F32)
        persist("dtb_col", [32, 1], F32)
        persist("d_bc", [128, 32], F32)
        persist("bias_bc", [1, DM], BF16)
        persist("cdp_log", [32, NCK], F32)
        persist("cdp_bc", [128, 512], F32)
        persist("p_init", [128, 2048], F32)
        persist("combA", [8, 8], F32)
        persist("combB", [8, 1], F32)
        persist("eps_col", [128, 1], F32)
        ctx.P = P

        PH = ctx.sem("PH")
        phase_idx = [0]

        def run_phase(emitters):
            k = phase_idx[0]
            for eng_name in ["sync", "tensor", "scalar", "vector", "gpsimd"]:
                fn = emitters.get(eng_name)

                def body(eng, fn=fn, k=k, eng_name=eng_name):
                    eng.wait_ge(PH, 5 * k)
                    if fn is not None:
                        fn(eng)
                    for sname, sh in ctx.sems.items():
                        if sname == "PH":
                            continue
                        if ctx.cnt.get(sname, 0) > 0:
                            eng.wait_ge(sh, ctx.cnt[sname])
                    eng.sem_inc(PH, 1)

                getattr(block, eng_name)(body)
            phase_idx[0] += 1

        builders = {"Q": phase_Q, "A1": phase_A1, "A2": phase_A2, "X": phase_X,
                    "B1": phase_B1, "B2": phase_B2, "B3": phase_B3}
        for ph in range(lo, hi + 1):
            with ExitStack() as pstack:
                ctx.pstack = pstack
                run_phase(builders[PHASES[ph]](ctx))
        if dump:
            with ExitStack() as dstack:
                dtile = dstack.enter_context(nc.sbuf_tensor("dumpt", [128, 8448], BF16))

                def dump_fn(gpsimd):
                    DS_ = ctx.sem("LG")
                    n = ctx.cnt["LG"]
                    for name, dst in dump.items():
                        shape, dtype, _ = SCRATCH[name]
                        rows, cols = shape[-2], shape[-1]
                        if len(shape) == 3:
                            rows = shape[0] * shape[1]
                        src2 = dr[name].rearrange("... -> ...") if len(shape) == 2 else dr[name]
                        bpe = 2 if dtype == BF16 else 4
                        csize = cols * bpe
                        view = dtile[:].bitcast(dtype)
                        for r0 in range(0, rows, 128):
                            rr = min(128, rows - r0)
                            if len(shape) == 2:
                                sap = dr[name][r0:r0 + rr, :]
                                dap = dst[r0:r0 + rr, :]
                            else:
                                a = r0 // shape[1]
                                sap = dr[name][a, r0 - a * shape[1]:r0 - a * shape[1] + rr, :]
                                dap = dst[a, r0 - a * shape[1]:r0 - a * shape[1] + rr, :]
                            gpsimd.dma_start(out=view[0:rr, 0:cols], in_=sap).then_inc(DS_, 16)
                            n += 16
                            gpsimd.wait_ge(DS_, n)
                            gpsimd.dma_start(out=dap, in_=view[0:rr, 0:cols]).then_inc(DS_, 16)
                            n += 16
                            gpsimd.wait_ge(DS_, n)
                    ctx.cnt["LG"] = n
                run_phase({"gpsimd": dump_fn})
    return nc


def phase_Q(ctx):
    nc, st, P = ctx.nc, ctx.pstack, ctx.P
    inp, d = ctx.inp, ctx.d
    LD, LG, EV, ES, MM = (ctx.sem(n) for n in ("LD", "LG", "EV", "ES", "MM"))
    QW, QF, QS, QG, GP = (ctx.sem(n) for n in ("QW", "QF", "QS", "QG", "GP"))
    c = ctx.snap()

    wtile = st.enter_context(nc.sbuf_tensor("q_wtile", [128, DI], F32))
    wofold = st.enter_context(nc.sbuf_tensor("q_wofold", [128, DI], BF16))
    nw_bc = st.enter_context(nc.sbuf_tensor("q_nwbc", [128, DI], F32))
    nw_row = st.enter_context(nc.sbuf_tensor("q_nwrow", [1, DI], F32))
    fwt = st.enter_context(nc.sbuf_tensor("q_fwt", [128, DM], F32))
    qt = st.enter_context(nc.sbuf_tensor("q_qt", [128, DM], BF16))
    thr = st.enter_context(nc.sbuf_tensor("q_thr", [128, 1], F32))
    thrn = st.enter_context(nc.sbuf_tensor("q_thrn", [128, 1], F32))
    qa = st.enter_context(nc.sbuf_tensor("q_qa", [128, DM], F32))
    qb = st.enter_context(nc.sbuf_tensor("q_qb", [128, DM], F32))
    fb_row = st.enter_context(nc.sbuf_tensor("q_fbrow", [1, DM], F32))
    alog = st.enter_context(nc.sbuf_tensor("q_alog", [32, 1], F32))
    aexp = st.enter_context(nc.sbuf_tensor("q_aexp", [32, 1], F32))
    drow = st.enter_context(nc.sbuf_tensor("q_drow", [1, 32], F32))
    psA = st.enter_context(nc.psum_tensor("q_psA", [128, 512], F32))
    psB = st.enter_context(nc.psum_tensor("q_psB", [128, 512], F32))

    NWT, FWT = 16, 16
    NLOAD0 = 14

    def sync_fn(sync):
        for name in ["ident128", "onesb_1x64", "onesb_128x1", "onesb_1x128",
                     "onesf_1x128", "onesf_1x64", "ones32"]:
            sync.dma_start(out=P[name][:], in_=inp[name]).then_inc(LD, 16)
        sync.dma_start(out=P["dtb_col"][:], in_=inp["dt_bias"]).then_inc(LD, 16)
        sync.dma_start(out=P["combA"][:], in_=inp["combA"]).then_inc(LD, 16)
        sync.dma_start(out=P["combB"][:], in_=inp["combB"]).then_inc(LD, 16)
        sync.dma_start(out=alog[:], in_=inp["a_log"]).then_inc(LD, 16)
        sync.dma_start(out=drow[:], in_=inp["d_vec"]).then_inc(LD, 16)
        sync.dma_start(out=nw_row[:], in_=inp["norm_w"]).then_inc(LD, 16)
        sync.dma_start(out=fb_row[:], in_=inp["ffn_b"]).then_inc(LD, 16)
        for i in range(NWT):
            if i >= 1:
                sync.wait_ge(EV, c["EV"] + 9 + i)
            sync.dma_start(out=wtile[:], in_=inp["w_out"][i * 128:(i + 1) * 128, :]).then_inc(QW, 16)
        for i in range(FWT):
            if i >= 1:
                sync.wait_ge(EV, c["EV"] + 9 + NWT + 5 * i)
            sync.dma_start(out=fwt[:], in_=inp["ffn_w"][i * 128:(i + 1) * 128, :]).then_inc(QF, 16)

    def tensor_fn(tensor):
        tensor.wait_ge(LD, c["LD"] + 16 * NLOAD0)
        for j in range(8):
            if j > 0:
                tensor.wait_ge(EV, c["EV"] + j)
            tensor.matmul(psA[:, 0:512], P["onesf_1x128"][:], nw_row[0:1, j * 512:(j + 1) * 512],
                          start=True, stop=True).then_inc(MM, 1)
        tensor.matmul(psB[:, 0:32], P["onesf_1x128"][:], drow[:], start=True, stop=True).then_inc(MM, 1)

    def scalar_fn(scalar):
        scalar.wait_ge(LD, c["LD"] + 16 * NLOAD0)
        scalar.activation(aexp[:], alog[:], AF.Exp).then_inc(ES, 1)

    def vector_fn(vector):
        for j in range(8):
            vector.wait_ge(MM, c["MM"] + j + 1)
            vector.tensor_copy(nw_bc[:, j * 512:(j + 1) * 512], psA[:, 0:512]).then_inc(EV, 1)
        vector.wait_ge(MM, c["MM"] + 9)
        vector.tensor_copy(P["d_bc"][:], psB[:, 0:32]).then_inc(EV, 1)
        vector.wait_ge(EV, c["EV"] + 9)
        for i in range(NWT):
            vector.wait_ge(QW, c["QW"] + 16 * (i + 1))
            if i >= 1:
                vector.wait_ge(QS, c["QS"] + 16 * i)
            vector.tensor_mul(wofold[:], wtile[:], nw_bc[:]).then_inc(EV, 1)
        for i in range(FWT):
            vector.wait_ge(QF, c["QF"] + 16 * (i + 1))
            if i >= 1:
                vector.wait_ge(QG, c["QG"] + 16 * i)
            vector.tensor_reduce(thr[:], fwt[:], AX.X, ALU.add,
                                 apply_absolute_value=True).then_inc(EV, 1)
            vector.wait_ge(EV, c["EV"] + 9 + NWT + 5 * i + 1)
            vector.tensor_scalar(thr[:], thr[:], 0.7 / DM, None, ALU.mult).then_inc(EV, 1)
            vector.wait_ge(EV, c["EV"] + 9 + NWT + 5 * i + 2)
            vector.tensor_scalar(thrn[:], thr[:], -1.0, None, ALU.mult).then_inc(EV, 1)
            vector.wait_ge(EV, c["EV"] + 9 + NWT + 5 * i + 3)
            vector.tensor_single_scalar(qa[:], fwt[:], thr[:], ALU.is_gt)
            vector.tensor_single_scalar(qb[:], fwt[:], thrn[:], ALU.is_lt).then_inc(EV, 1)
            vector.wait_ge(EV, c["EV"] + 9 + NWT + 5 * i + 4)
            vector.tensor_sub(qt[:], qa[:], qb[:]).then_inc(EV, 1)
        vector.tensor_copy(P["bias_bc"][:], fb_row[:]).then_inc(EV, 1)
        vector.memset(P["p_init"][:, :], 0.0).then_inc(EV, 1)
        vector.memset(P["cdp_log"][:, :], 0.0).then_inc(EV, 1)
        vector.memset(P["eps_col"][:, :], EPS).then_inc(EV, 1)

    def gpsimd_fn(gpsimd):
        for i in range(8):
            gpsimd.dma_start(out=d["w_in_bf"][i * 1044:(i + 1) * 1044, :],
                             in_=inp["w_in"][i * 1044:(i + 1) * 1044, :]).then_inc(LG, 16)
        for i in range(4):
            gpsimd.dma_start(out=d["u_bf"][i * 1056:(i + 1) * 1056, :],
                             in_=inp["u_pad"][i * 1056:(i + 1) * 1056, :]).then_inc(LG, 16)
        for i in range(NWT):
            gpsimd.wait_ge(EV, c["EV"] + 9 + i + 1)
            gpsimd.dma_start(out=d["wo_sp"][i * 128:(i + 1) * 128, :],
                             in_=wofold[:]).then_inc(QS, 16)
        for i in range(FWT):
            gpsimd.wait_ge(EV, c["EV"] + 9 + NWT + 5 * (i + 1))
            gpsimd.dma_start(out=d["q_sp"][i * 128:(i + 1) * 128, :],
                             in_=qt[:]).then_inc(QG, 16)
        gpsimd.wait_ge(ES, c["ES"] + 1)
        gpsimd.tensor_scalar(P["a_col"][:], aexp[:], -1.0, None, ALU.mult).then_inc(GP, 1)

    ctx.cnt["LD"] = c["LD"] + 16 * NLOAD0
    ctx.cnt["QW"] = c["QW"] + 16 * NWT
    ctx.cnt["QF"] = c["QF"] + 16 * FWT
    ctx.cnt["QS"] = c["QS"] + 16 * NWT
    ctx.cnt["QG"] = c["QG"] + 16 * FWT
    ctx.cnt["LG"] = c["LG"] + 16 * (8 + 4)
    ctx.cnt["GP"] = c["GP"] + 1
    ctx.cnt["EV"] = c["EV"] + 9 + NWT + 5 * FWT + 4
    ctx.cnt["ES"] = c["ES"] + 1
    ctx.cnt["MM"] = c["MM"] + 9
    return {"sync": sync_fn, "tensor": tensor_fn, "scalar": scalar_fn,
            "vector": vector_fn, "gpsimd": gpsimd_fn}


def phase_A1(ctx):
    nc, st, P = ctx.nc, ctx.pstack, ctx.P
    inp, d = ctx.inp, ctx.d
    LD, LS, LG, MM, EV, ES = (ctx.sem(n) for n in ("LD", "LS", "LG", "MM", "EV", "ES"))
    VX = ctx.sem("VX")
    AU = ctx.sem("AU")
    AW = [ctx.sem("AW0"), ctx.sem("AW1")]
    AX = [ctx.sem(f"AX{i}") for i in range(4)]
    c = ctx.snap()

    uT = [st.enter_context(nc.sbuf_tensor(f"a1_uT{k}", [128, 2176], BF16)) for k in range(16)]
    WT = [[st.enter_context(nc.sbuf_tensor(f"a1_WT{bu}_{k}", [128, 768], BF16))
           for k in range(16)] for bu in range(2)]
    WTdt = [st.enter_context(nc.sbuf_tensor(f"a1_WTdt{k}", [128, 32], BF16)) for k in range(16)]
    wdt_rows = st.enter_context(nc.sbuf_tensor("a1_wdtrows", [32, DM], BF16))
    cw_t = st.enter_context(nc.sbuf_tensor("a1_cw", [128, 4 * 33], F32))
    cb_t = st.enter_context(nc.sbuf_tensor("a1_cb", [128, 33], F32))
    raw_s = st.enter_context(nc.sbuf_tensor("a1_raw", [128, 515], F32))
    carry = st.enter_context(nc.sbuf_tensor("a1_carry", [128, 3 * 33], F32))
    conv_o = st.enter_context(nc.sbuf_tensor("a1_convo", [128, 512], F32))
    xg_stage = [st.enter_context(nc.sbuf_tensor(f"a1_xg{i}", [128, 512], BF16)) for i in range(4)]
    dt_stage = st.enter_context(nc.sbuf_tensor("a1_dts", [32, TP], F32))
    dtb_stage = st.enter_context(nc.sbuf_tensor("a1_dtbs", [32, TP], BF16))
    dtt_stage = st.enter_context(nc.sbuf_tensor("a1_dtts", [128, 33 * 32], F32))
    dt_tmp = st.enter_context(nc.sbuf_tensor("a1_dttmp", [32, 512], F32))
    dt_tmp2 = st.enter_context(nc.sbuf_tensor("a1_dttmp2", [32, 512], F32))
    dt_tmp3 = st.enter_context(nc.sbuf_tensor("a1_dttmp3", [32, 512], F32))
    psum = [st.enter_context(nc.psum_tensor(f"a1_ps{i}", [128, 512], F32)) for i in range(4)]
    psT = st.enter_context(nc.psum_tensor("a1_psT", [128, 512], BF16))

    HALFTGS = [[0, 1, 2, 3], [4, 5, 6, 7, 8]]
    HW = [0, 2048]
    UNITS = []
    for half in (0, 1):
        for gi, eg in enumerate(EGROUPS):
            for et in eg:
                for tg in HALFTGS[half]:
                    UNITS.append((len(UNITS), half, half * NGRP + gi, et, tg))
        for tg in HALFTGS[half]:
            UNITS.append((len(UNITS), half, None, 65, tg))
    if A1_LIMIT is not None:
        UNITS = [u for u in UNITS[:A1_LIMIT]]
    HAS_DT = any(u[3] == 65 for u in UNITS)
    NU = len(UNITS)

    def owner(et):
        return "s" if et < 32 else "v"

    x_units = [u for u in UNITS if 32 <= u[3] < 65]
    z_units = [u for u in UNITS if u[3] < 32]
    dt_units = [u for u in UNITS if u[3] == 65]
    xpos = {u[0]: k for k, u in enumerate(x_units)}
    dtpos = {u[0]: k for k, u in enumerate(dt_units)}
    es_rank = {i: i + 1 for i in range(NU)}

    _ld_events = {}

    def _build_ld_sched():
        pg = sorted({u[2] for u in UNITS if u[2] is not None})
        hp = sorted({u[1] for u in UNITS})
        nu = 0
        nw = [0, 0]
        for half in hp:
            nu += 16
            _ld_events[("uT", half)] = c["AU"] + 16 * nu
            for gi in range(NGRP):
                gg = half * NGRP + gi
                if gg not in pg:
                    continue
                bu = gg % 2
                nw[bu] += 16
                _ld_events[("W", gg)] = (bu, c[f"AW{bu}"] + 16 * nw[bu])
        _ld_events["ut_total"] = c["AU"] + 16 * nu
        _ld_events["w_total"] = (c["AW0"] + 16 * nw[0], c["AW1"] + 16 * nw[1])

    def ld_after_uT(half):
        return _ld_events[("uT", half)]

    last_unit_of_group = {}
    for (i, half, gg, et, tg) in UNITS:
        if gg is not None:
            last_unit_of_group[gg] = i
    _build_ld_sched()

    def mm_after_group(gg):
        return c["MM"] + NWTDT + last_unit_of_group[gg] + 1

    N_H0_UNITS = 65 * 4 + 4

    present_groups = sorted({u[2] for u in UNITS if u[2] is not None})
    halves_present = sorted({u[1] for u in UNITS})

    def sync_fn(sync):
        if A1_PARAMS_ON_SYNC:
            scalarq_loads(sync)
        for half in halves_present:
            co = HW[half]
            w = 2048 if half == 0 else 2176
            if half == 1:
                sync.wait_ge(MM, c["MM"] + NWTDT + N_H0_UNITS)
            for k in range(16):
                if A1_NO_ULOAD:
                    sync.dma_start(out=uT[k][0:1, 0:16], in_=d["u_bf"][0:1, 0:16]).then_inc(AU, 16)
                else:
                    sync.dma_start(out=uT[k][:, 0:w],
                                   in_=d["u_bf"][co:co + w, k * 128:(k + 1) * 128],
                                   transpose=True).then_inc(AU, 16)
            for gi, eg in enumerate(EGROUPS):
                gg = half * NGRP + gi
                if gg not in present_groups:
                    continue
                e0 = eg[0] * 128
                ew = 128 * len(eg)
                bu = gg % 2
                if gg >= 2 and (gg - 2) in last_unit_of_group:
                    sync.wait_ge(MM, mm_after_group(gg - 2))
                for k in range(16):
                    sync.dma_start(out=WT[bu][k][:, 0:ew],
                                   in_=d["w_in_bf"][e0:e0 + ew, k * 128:(k + 1) * 128],
                                   transpose=True).then_inc(AW[bu], 16)

    def scalarq_loads(scalar):
        for i in range(33):
            scalar.dma_start(out=cw_t[:, 4 * i:4 * i + 4],
                             in_=inp["conv_w"][i * 128:(i + 1) * 128, :]).then_inc(LS, 16)
            scalar.dma_start(out=cb_t[:, i:i + 1],
                             in_=inp["conv_b"][i * 128:(i + 1) * 128, :]).then_inc(LS, 16)
        scalar.dma_start(out=wdt_rows[:], in_=d["w_in_bf"][8320:8352, :]).then_inc(LS, 16)

    NWTDT = 0 if A1_NO_WTDT else 16

    def tensor_fn(tensor):
        tensor.wait_ge(LS, c["LS"] + 16 * 67)
        for k in range(NWTDT):
            if k > 0:
                tensor.wait_ge(EV, c["EV"] + 1 + k)
            tensor.transpose(psT[0:128, 0:32], wdt_rows[:, k * 128:(k + 1) * 128],
                             P["ident128"][0:32, 0:32]).then_inc(MM, 1)
        for (i, half, gg, et, tg) in UNITS:
            co = HW[half]
            tgo, tgw = TGS[tg]
            bank = i % 4
            if i >= 4:
                tensor.wait_ge(ES, c["ES"] + (i - 4) + 1)
            if A1_NO_MM:
                pass
            elif et == 65:
                tensor.wait_ge(EV, c["EV"] + 1 + NWTDT)
                tensor.wait_ge(AU, ld_after_uT(half))
                last = None
                for k in range(16):
                    last = tensor.matmul(psum[bank][0:32, 0:tgw], WTdt[k][:, 0:32],
                                         uT[k][:, tgo - co:tgo - co + tgw],
                                         start=(k == 0), stop=(k == 15))
            else:
                gi_local = gg - half * NGRP
                eg = EGROUPS[gi_local]
                bu = gg % 2
                eoff = (et - eg[0]) * 128
                tensor.wait_ge(AU, ld_after_uT(half))
                bu2, wmark = _ld_events[("W", gg)]
                tensor.wait_ge(AW[bu2], wmark)
                KK = A1_MM_K
                last = None
                for k in range(KK):
                    last = tensor.matmul(psum[bank][:, 0:tgw],
                                         WT[bu][k][:, eoff:eoff + 128],
                                         uT[k][:, tgo - co:tgo - co + tgw],
                                         start=(k == 0), stop=(k == KK - 1))
            if A1_NO_MM or last is None:
                tensor.nop().then_inc(MM, 1)
            else:
                last.then_inc(MM, 1)
        if not HAS_DT:
            return
        tensor.wait_ge(EV, ev_after_dtcast())
        for t in range(33):
            if t > 0:
                tensor.wait_ge(EV, ev_after_dtcast() + t)
            tensor.transpose(psT[0:128, 32:64], dtb_stage[:, t * 128:(t + 1) * 128],
                             P["ident128"][0:32, 0:32]).then_inc(MM, 1)

    ev_sched = {}

    def build_ev_sched():
        n = c["EV"] + 1 + (0 if A1_NO_WTDT else 16)
        for u in UNITS:
            i, _, _, et, _ = u
            if 32 <= et < 65:
                n += 4
                ev_sched[("x", i)] = n
            elif et == 65:
                n += 2
                ev_sched[("dt", i)] = n
        if HAS_DT:
            ev_sched["dtcast"] = n + 1
            ev_sched["end"] = n + 1 + 33
        else:
            ev_sched["dtcast"] = n
            ev_sched["end"] = n
    build_ev_sched()

    def ev_after_xunit(i):
        return ev_sched[("x", i)]

    def ev_after_dtcast():
        return ev_sched["dtcast"]

    def vector_fn(vector):
        vector.memset(carry[:, :], 0.0).then_inc(EV, 1)
        vector.wait_ge(EV, c["EV"] + 1)
        vector.wait_ge(LS, c["LS"] + 16 * 67)
        for k in range(NWTDT):
            vector.wait_ge(MM, c["MM"] + k + 1)
            vector.tensor_copy(WTdt[k][:], psT[0:128, 0:32]).then_inc(EV, 1)
        for (i, half, gg, et, tg) in UNITS:
            bank = i % 4
            tgo, tgw = TGS[tg]
            if 32 <= et < 65:
                ei = et - 32
                vector.wait_ge(MM, c["MM"] + NWTDT + i + 1)
                if xpos[i] > 0:
                    vector.wait_ge(ES, c["ES"] + es_rank[x_units[xpos[i] - 1][0]])
                vector.tensor_copy(raw_s[:, 0:3], carry[:, 3 * ei:3 * ei + 3]).then_inc(EV, 1)
                vector.tensor_copy(raw_s[:, 3:3 + tgw], psum[bank][:, 0:tgw]).then_inc(EV, 1)
                vector.wait_ge(EV, ev_sched[("x", i)] - 2)
                vector.tensor_copy(carry[:, 3 * ei:3 * ei + 3], raw_s[:, tgw:tgw + 3]).then_inc(EV, 1)
                vx0 = c["VX"] + 3 * xpos[i]
                vector.tensor_scalar(conv_o[:, 0:tgw], raw_s[:, 0:tgw],
                                     cw_t[:, 4 * ei:4 * ei + 1], cb_t[:, ei:ei + 1],
                                     ALU.mult, ALU.add).then_inc(VX, 1)
                for k in range(1, 3):
                    vector.wait_ge(VX, vx0 + k)
                    vector.scalar_tensor_tensor(conv_o[:, 0:tgw], raw_s[:, k:k + tgw],
                                                cw_t[:, 4 * ei + k:4 * ei + k + 1],
                                                conv_o[:, 0:tgw], ALU.mult,
                                                ALU.add).then_inc(VX, 1)
                vector.wait_ge(VX, vx0 + 3)
                vector.scalar_tensor_tensor(conv_o[:, 0:tgw], raw_s[:, 3:3 + tgw],
                                            cw_t[:, 4 * ei + 3:4 * ei + 4],
                                            conv_o[:, 0:tgw], ALU.mult,
                                            ALU.add).then_inc(EV, 1)
            elif et == 65:
                vector.wait_ge(MM, c["MM"] + NWTDT + i + 1)
                if dtpos[i] > 0:
                    vector.wait_ge(ES, c["ES"] + es_rank[dt_units[dtpos[i] - 1][0]])
                vector.tensor_scalar(dt_tmp[:, 0:tgw], psum[bank][0:32, 0:tgw],
                                     P["dtb_col"][:], None, ALU.add).then_inc(EV, 1)
                vector.wait_ge(ES, c["ES"] + es_rank[i])
                vector.tensor_add(dt_stage[:, tgo:tgo + tgw], dt_tmp2[:, 0:tgw],
                                  dt_tmp3[:, 0:tgw]).then_inc(EV, 1)
        if not HAS_DT:
            return
        vector.tensor_copy(dtb_stage[:, :], dt_stage[:, :]).then_inc(EV, 1)
        for t in range(33):
            vector.wait_ge(MM, c["MM"] + NWTDT + NU + t + 1)
            vector.tensor_copy(dtt_stage[:, 32 * t:32 * t + 32],
                               psT[0:128, 32:64]).then_inc(EV, 1)

    def scalar_fn(scalar):
        if not A1_PARAMS_ON_SYNC:
            scalarq_loads(scalar)
        for (i, half, gg, et, tg) in UNITS:
            bank = i % 4
            tgo, tgw = TGS[tg]
            sidx = i % 4
            if et < 32:
                scalar.wait_ge(MM, c["MM"] + NWTDT + i + 1)
                if i >= 4 and not A1_NO_SPILL:
                    scalar.wait_ge(AX[(i - 4) % 4], ax_after_unit(i - 4))
                if A1_NO_SILU:
                    scalar.copy(xg_stage[sidx][:, 0:tgw], psum[bank][:, 0:tgw]).then_inc(ES, 1)
                else:
                    scalar.activation(xg_stage[sidx][:, 0:tgw], psum[bank][:, 0:tgw],
                                      AF.Silu).then_inc(ES, 1)
            elif et < 65:
                scalar.wait_ge(EV, ev_after_xunit(i))
                if i >= 4 and not A1_NO_SPILL:
                    scalar.wait_ge(AX[(i - 4) % 4], ax_after_unit(i - 4))
                scalar.activation(xg_stage[sidx][:, 0:tgw], conv_o[:, 0:tgw],
                                  AF.Silu).then_inc(ES, 1)
            else:
                j = dtpos[i]
                tgo, tgw = TGS[tg]
                scalar.wait_ge(EV, ev_sched[("dt", i)] - 1)
                if j > 0:
                    scalar.wait_ge(EV, ev_sched[("dt", dt_units[j - 1][0])])
                scalar.activation(dt_tmp2[:, 0:tgw], dt_tmp[:, 0:tgw], AF.Abs)
                scalar.activation(dt_tmp2[:, 0:tgw], dt_tmp2[:, 0:tgw], AF.Exp, scale=-1.0)
                scalar.activation(dt_tmp2[:, 0:tgw], dt_tmp2[:, 0:tgw], AF.Ln, bias=1.0)
                scalar.activation(dt_tmp3[:, 0:tgw], dt_tmp[:, 0:tgw], AF.Relu).then_inc(ES, 1)

    slot_dma = [0] * NU
    _slot_tot = [0, 0, 0, 0]
    for (i, _, _, et, _) in UNITS:
        sl = i % 4
        if et < 64:
            _slot_tot[sl] += 1
        elif et == 64:
            _slot_tot[sl] += 2
        slot_dma[i] = _slot_tot[sl]

    def ax_after_unit(i):
        return c[f"AX{i % 4}"] + 16 * slot_dma[i]

    def gpsimd_fn(gpsimd):
        if A1_NO_SPILL:
            return
        for (i, half, gg, et, tg) in UNITS:
            if et == 65:
                continue
            sidx = i % 4
            tgo, tgw = TGS[tg]
            gpsimd.wait_ge(ES, c["ES"] + es_rank[i])
            if et < 32:
                gpsimd.dma_start(out=d["g_sp"][et * 128:(et + 1) * 128, tgo:tgo + tgw],
                                 in_=xg_stage[sidx][:, 0:tgw]).then_inc(AX[sidx], 16)
            elif et < 64:
                gpsimd.dma_start(out=d["x_sp"][(et - 32) * 128:(et - 31) * 128, tgo:tgo + tgw],
                                 in_=xg_stage[sidx][:, 0:tgw]).then_inc(AX[sidx], 16)
            else:
                gpsimd.dma_start(out=d["b_sp"][:, tgo:tgo + tgw],
                                 in_=xg_stage[sidx][0:64, 0:tgw]).then_inc(AX[sidx], 16)
                gpsimd.dma_start(out=d["c_sp"][:, tgo:tgo + tgw],
                                 in_=xg_stage[sidx][64:128, 0:tgw]).then_inc(AX[sidx], 16)
        if HAS_DT and not A1_NO_SPILL:
            gpsimd.wait_ge(EV, ev_sched["end"])
            gpsimd.dma_start(out=d["dt_sp"][:, :], in_=dt_stage[:, :]).then_inc(LG, 16)
            gpsimd.dma_start(out=d["dttm_sp"][:, :], in_=dtt_stage[:, :]).then_inc(LG, 16)


    ctx.cnt["AU"] = _ld_events["ut_total"]
    ctx.cnt["AW0"], ctx.cnt["AW1"] = _ld_events["w_total"]
    ctx.cnt["LS"] = c["LS"] + 16 * 67
    if not A1_NO_SPILL:
        for sl in range(4):
            ctx.cnt[f"AX{sl}"] = c[f"AX{sl}"] + 16 * _slot_tot[sl]
        ctx.cnt["LG"] = c["LG"] + (32 if HAS_DT else 0)
    ctx.cnt["MM"] = c["MM"] + NWTDT + NU + (33 if HAS_DT else 0)
    ctx.cnt["EV"] = ev_sched["end"]
    ctx.cnt["ES"] = c["ES"] + NU
    ctx.cnt["VX"] = c["VX"] + 3 * len(x_units)
    return {"sync": sync_fn, "tensor": tensor_fn, "scalar": scalar_fn,
            "vector": vector_fn, "gpsimd": gpsimd_fn}



def phase_A2(ctx):
    nc, st, P = ctx.nc, ctx.pstack, ctx.P
    inp, d = ctx.inp, ctx.d
    LD, LS, LG, MM, EV, ES = (ctx.sem(n) for n in ("LD", "LS", "LG", "MM", "EV", "ES"))
    AD = [ctx.sem("AD0"), ctx.sem("AD1")]
    GA, GB, GE, GR, GL, GC, GS = (ctx.sem(n) for n in ("GA", "GB", "GE", "GR", "GL", "GC", "GS"))
    GY = [ctx.sem("GY0"), ctx.sem("GY1")]
    c0 = ctx.snap()

    x_tm = [st.enter_context(nc.sbuf_tensor(f"a2_xtm{z}", [128, DI], BF16))
            for z in range(2)]
    expc_row = st.enter_context(nc.sbuf_tensor("a2_ecr", [1, HS], BF16))
    xT = st.enter_context(nc.sbuf_tensor("a2_xT", [128, HS], BF16))
    b_bf = [st.enter_context(nc.sbuf_tensor(f"a2_b{p}", [64, 256], BF16)) for p in range(2)]
    c_bf = [st.enter_context(nc.sbuf_tensor(f"a2_c{p}", [64, 256], BF16)) for p in range(2)]
    dthm = st.enter_context(nc.sbuf_tensor("a2_dt", [32, 256], F32))
    dttm = [st.enter_context(nc.sbuf_tensor(f"a2_dttm{z}", [128, 32], F32))
            for z in range(2)]
    dA = st.enter_context(nc.sbuf_tensor("a2_dA", [32, 256], F32))
    cum = st.enter_context(nc.sbuf_tensor("a2_cum", [32, 256], F32))
    cumlast = st.enter_context(nc.sbuf_tensor("a2_cl", [32, 1], F32))
    ctot = [st.enter_context(nc.sbuf_tensor(f"a2_ct{p}", [32, 1], F32)) for p in range(2)]
    clrow = st.enter_context(nc.sbuf_tensor("a2_clrow", [1, 32], F32))
    decrow = st.enter_context(nc.sbuf_tensor("a2_decrow", [1, 32], F32))
    negcum = st.enter_context(nc.sbuf_tensor("a2_negcum", [66, 256], BF16))
    rhs_seg = st.enter_context(nc.sbuf_tensor("a2_rhs", [66, HS], BF16))
    maskU = [st.enter_context(nc.sbuf_tensor(f"a2_mU{z}", [128, 128], BF16)) for z in range(2)]
    maskV = [st.enter_context(nc.sbuf_tensor(f"a2_mV{z}", [128, 512], BF16)) for z in range(2)]
    expseg = st.enter_context(nc.sbuf_tensor("a2_expseg", [128, HS], BF16))
    M = [st.enter_context(nc.sbuf_tensor(f"a2_M{z}", [128, HS], BF16)) for z in range(2)]
    scoresT = st.enter_context(nc.sbuf_tensor("a2_scT", [128, 512], BF16))
    B_tmS = [st.enter_context(nc.sbuf_tensor(f"a2_Btm{z}", [128, 64], BF16)) for z in range(2)]
    dec_tmS = [st.enter_context(nc.sbuf_tensor(f"a2_dtm{z}", [128, 32], F32)) for z in range(2)]
    xdt = [st.enter_context(nc.sbuf_tensor(f"a2_xdt{z}", [128, DI], BF16)) for z in range(2)]
    Bdec = [st.enter_context(nc.sbuf_tensor(f"a2_Bd{z}", [128, NH * 64], BF16)) for z in range(2)]
    Cs = st.enter_context(nc.sbuf_tensor("a2_Cs", [128, 2 * DM], BF16))
    decay_hm = st.enter_context(nc.sbuf_tensor("a2_dhm", [32, 256], BF16))
    expcum_hm = st.enter_context(nc.sbuf_tensor("a2_ehm", [32, 256], BF16))
    dec_bc = st.enter_context(nc.sbuf_tensor("a2_dbc", [128, 32], F32))
    y_stage = [st.enter_context(nc.sbuf_tensor(f"a2_yst{i}", [128, 2048], BF16)) for i in range(2)]
    PP = [st.enter_context(nc.sbuf_tensor(f"a2_P{p}", [128, 2048], F32)) for p in range(2)]
    P_bf = st.enter_context(nc.sbuf_tensor("a2_Pbf", [128, 2048], BF16))
    psSEG = [st.enter_context(nc.psum_tensor(f"a2_psSG{i}", [128, 512], F32)) for i in range(2)]
    psY = [st.enter_context(nc.psum_tensor(f"a2_psY{i}", [128, 512], F32)) for i in range(2)]
    psS = st.enter_context(nc.psum_tensor("a2_psS", [128, 512], F32))
    psCB = [st.enter_context(nc.psum_tensor(f"a2_psCB{i}", [128, 512], F32)) for i in range(2)]
    psTR = st.enter_context(nc.psum_tensor("a2_psTR", [128, 1024], BF16))
    psSC = psCB[0]

    EVC, ESC, MMC, LDC, LGC = 225, 35, 118, 8 * 16, 12 * 16
    PRE_LD = 6 * 16
    PRE_EV = 1

    def mkld(c):
        return c0[f"AD{c % 2}"] + LDC * (c // 2 + 1)

    def ev0(c):
        return c0["EV"] + PRE_EV + EVC * c

    def es0(c):
        return c0["ES"] + ESC * c

    def mm0(c):
        return c0["MM"] + MMC * c

    def lg0(c):
        return c0["LG"] + LGC * c

    EVO = dict(cum=2, nh=3, t1=4, nl=5, cl=6, cdp=7, ct=8, pbf=9, sct=10, dbc=11,
               btm0=12, btm1=13, dtm0=14, dtm1=15)
    EVO_XDT0 = 15
    EVO_BD0 = 79
    EVO_M0 = 143
    EVO_CS0 = 145
    EVO_Y0 = 161

    ESO = dict(dhm=1, ehm=2, dcr=3)
    ESO_EXP0 = 3

    MMO = dict(sc=1, dec=2, tr0=3, tr1=4, tr2=5, tr3=6)
    MMO_SEG0 = 6
    MMO_CB0 = 38
    MMO_H0 = 54
    def mm_seg_done(c):
        return mm0(c) + MMO_SEG0 + 32

    def mm_cs_done(c):
        return mm0(c) + MMO_CB0 + 16

    def mm_tr_done(c):
        return mm0(c) + MMO["tr3"]

    def mm_heads_done(c):
        return mm0(c) + MMC


    def sync_fn(sync):
        sync.dma_start(out=maskU[0][:], in_=inp["maskU1"]).then_inc(LD, 16)
        sync.dma_start(out=maskV[0][:], in_=inp["maskV1w"]).then_inc(LD, 16)
        sync.dma_start(out=maskU[1][:], in_=inp["maskU2"]).then_inc(LD, 16)
        sync.dma_start(out=maskV[1][:], in_=inp["maskV2w"]).then_inc(LD, 16)
        sync.dma_start(out=rhs_seg[0:64, :], in_=inp["indicator"]).then_inc(LD, 16)
        sync.dma_start(out=negcum[64:66, :], in_=inp["negones2"]).then_inc(LD, 16)
        for c in range(NCK):
            p = c % 2
            t0 = CTX + 256 * c
            if c >= 2:
                sync.wait_ge(EV, ev0(c - 2) + EVC)
                sync.wait_ge(MM, mm_heads_done(c - 2))
            if c >= 1:
                sync.wait_ge(EV, ev0(c - 1) + EVO_XDT0 + 64)
            for z in range(2):
                sync.dma_start(out=x_tm[z][:, :],
                               in_=d["x_sp"][0:DI, t0 + 128 * z:t0 + 128 * (z + 1)],
                               transpose=True).then_inc(AD[p], 16)
            if c >= 1:
                sync.wait_ge(EV, ev0(c - 1) + EVO_Y0 + 64)
            sync.dma_start(out=xT[:].rearrange("p (t c) -> p t c", t=32),
                           in_=d["x_sp"][0:DI, t0:t0 + 256].rearrange("(t p) c -> p t c", p=128)
                           ).then_inc(AD[p], 16)
            sync.dma_start(out=b_bf[p][:], in_=d["b_sp"][:, t0:t0 + 256]).then_inc(AD[p], 16)
            sync.dma_start(out=c_bf[p][:], in_=d["c_sp"][:, t0:t0 + 256]).then_inc(AD[p], 16)
            if c >= 1:
                sync.wait_ge(EV, ev0(c - 1) + EVO["cum"])
            sync.dma_start(out=dthm[:], in_=d["dt_sp"][:, t0:t0 + 256]).then_inc(AD[p], 16)
            for z in range(2):
                blk = 1 + 2 * c + z
                sync.dma_start(out=dttm[z][:],
                               in_=d["dttm_sp"][:, 32 * blk:32 * blk + 32]).then_inc(AD[p], 16)

    def gpsimd_fn(gpsimd):
        for c in range(NCK):
            gpsimd.wait_ge(EV, ev0(c) + EVO["nh"])
            gpsimd.dma_start(out=d["nhl_sp"][c, 0:1, :].rearrange("a (h s) -> (a h) s", h=32),
                             in_=negcum[0:32, :]).then_inc(GA, 16)
            gpsimd.wait_ge(EV, ev0(c) + EVO["nl"])
            gpsimd.dma_start(out=d["nhl_sp"][c, 1:2, :].rearrange("a (h s) -> (a h) s", h=32),
                             in_=negcum[32:64, :]).then_inc(GA, 16)
            gpsimd.wait_ge(EV, ev0(c) + EVO["cl"])
            gpsimd.dma_start(out=d["cl_sp"][c:c + 1, :].rearrange("a h -> h a"),
                             in_=cumlast[:]).then_inc(GB, 16)
            gpsimd.wait_ge(ES, es0(c) + ESO["ehm"])
            gpsimd.dma_start(out=d["expc_sp"][c:c + 1, :].rearrange("a (h s) -> (a h) s", h=32),
                             in_=expcum_hm[:]).then_inc(GE, 16)
            if c >= 1:
                gpsimd.wait_ge(MM, mm_seg_done(c - 1))
            gpsimd.wait_ge(GA, c0["GA"] + 32 * (c + 1))
            gpsimd.dma_start(out=rhs_seg[64:66, :], in_=d["nhl_sp"][c]).then_inc(GR, 16)
            if c >= 1:
                gpsimd.wait_ge(ES, es0(c - 1) + ESO["dcr"])
            gpsimd.wait_ge(GB, c0["GB"] + 16 * (c + 1))
            gpsimd.dma_start(out=clrow[:], in_=d["cl_sp"][c:c + 1, :]).then_inc(GL, 16)
            if c >= 1:
                gpsimd.wait_ge(MM, mm_cs_done(c - 1))
            gpsimd.wait_ge(GE, c0["GE"] + 16 * (c + 1))
            gpsimd.dma_start(out=expc_row[:], in_=d["expc_sp"][c:c + 1, :]).then_inc(GC, 16)
            for hg in range(4):
                gpsimd.wait_ge(EV, ev0(c) + EVO_Y0 + 16 * (hg + 1) - 1)
                gpsimd.dma_start(out=d["y_sp"][c, :, 2048 * hg:2048 * (hg + 1)],
                                 in_=y_stage[hg % 2][:]).then_inc(GY[hg % 2], 16)
            gpsimd.wait_ge(EV, ev0(c) + EVO_CS0 + 16)
            gpsimd.dma_start(out=d["cs_sp"][c], in_=Cs[:]).then_inc(GS, 16)
        gpsimd.wait_ge(EV, ev0(NCK - 1) + EVC)
        gpsimd.dma_start(out=d["exch_in"][0:128, :], in_=PP[0][:]).then_inc(LG, 16)
        gpsimd.dma_start(out=d["exch_in"][128:129, 0:32].rearrange("a h -> h a"),
                         in_=ctot[NCK % 2][:]).then_inc(LG, 16)
        gpsimd.dma_start(out=d["exch_in"][128:129, 32:DM],
                         in_=PP[0][0:1, 32:DM]).then_inc(LG, 16)
        gpsimd.dma_start(out=d["exch_in"][129:130, :],
                         in_=PP[0][0:1, :]).then_inc(LG, 16)
        gpsimd.dma_start(out=d["cdp_sp"][:, :].rearrange("c h -> h c"),
                         in_=P["cdp_log"][:],
                         allow_slow_non_contiguous=True).then_inc(LG, 16)

    def vector_fn(vector):
        vector.memset(PP[0][:, :], 0.0).then_inc(EV, 1)
        vector.wait_ge(EV, c0["EV"] + 1)
        for c in range(NCK):
            p = c % 2
            base = ev0(c)
            vector.wait_ge(AD[p], mkld(c))
            if c >= 1:
                vector.wait_ge(GA, c0["GA"] + 32 * c)
                vector.wait_ge(MM, mm_seg_done(c - 1))
                vector.wait_ge(ES, es0(c - 1) + ESO["ehm"])
            vector.tensor_scalar(dA[:], dthm[:], P["a_col"][:], None, ALU.mult).then_inc(EV, 1)
            vector.tensor_tensor_scan(cum[:], P["ones32"][:], dA[:], 0.0,
                                      ALU.mult, ALU.add).then_inc(EV, 1)
            vector.wait_ge(EV, base + EVO["cum"])
            vector.tensor_scalar(negcum[0:32, :], cum[:], -1.0, None, ALU.mult).then_inc(EV, 1)
            vector.tensor_add(dA[:], cum[:], negcum[0:32, :]).then_inc(EV, 1)
            vector.tensor_scalar(negcum[32:64, :], dA[:], -1.0, None, ALU.mult).then_inc(EV, 1)
            vector.tensor_copy(cumlast[:], cum[:, 255:256]).then_inc(EV, 1)
            vector.wait_ge(EV, base + EVO["cl"])
            if c == 0:
                vector.tensor_scalar(P["cdp_log"][:, 0:1], cumlast[:], 0.0, None,
                                     ALU.mult).then_inc(EV, 1)
                vector.tensor_copy(ctot[1][:], cumlast[:]).then_inc(EV, 1)
            else:
                vector.tensor_copy(P["cdp_log"][:, c:c + 1], ctot[(c - 1) % 2 ^ 1][:]).then_inc(EV, 1)
                vector.tensor_add(ctot[c % 2 ^ 1][:], ctot[(c - 1) % 2 ^ 1][:],
                                  cumlast[:]).then_inc(EV, 1)
            vector.tensor_copy(P_bf[:], PP[c % 2][:]).then_inc(EV, 1)
            vector.wait_ge(MM, mm0(c) + MMO["sc"])
            vector.tensor_copy(scoresT[:], psSC[:]).then_inc(EV, 1)
            vector.wait_ge(MM, mm0(c) + MMO["dec"])
            vector.tensor_copy(dec_bc[:], psY[0][0:128, 0:32]).then_inc(EV, 1)
            for z in range(2):
                vector.wait_ge(MM, mm0(c) + MMO[f"tr{z}"])
                vector.tensor_copy(B_tmS[z][:], psTR[0:128, 64 * z:64 * z + 64]).then_inc(EV, 1)
            for z in range(2):
                vector.wait_ge(MM, mm0(c) + MMO[f"tr{2 + z}"])
                vector.tensor_copy(dec_tmS[z][:],
                                   psTR[0:128, 128 + 32 * z:160 + 32 * z]).then_inc(EV, 1)
            vector.wait_ge(EV, base + EVO["dtm1"])
            if c >= 1:
                vector.wait_ge(MM, mm_heads_done(c - 1))
            for z in range(2):
                for h in range(NH):
                    vector.tensor_scalar(xdt[z][:, 128 * h:128 * (h + 1)],
                                         x_tm[z][:, 128 * h:128 * (h + 1)],
                                         dttm[z][:, h:h + 1], None,
                                         ALU.mult).then_inc(EV, 1)
            for z in range(2):
                for h in range(NH):
                    vector.tensor_scalar(Bdec[z][:, 64 * h:64 * (h + 1)], B_tmS[z][:],
                                         dec_tmS[z][:, h:h + 1], None,
                                         ALU.mult).then_inc(EV, 1)
            for z in range(2):
                vector.wait_ge(ES, es0(c) + ESO_EXP0 + 16 * (z + 1))
                vector.tensor_tensor(M[z][:].rearrange("p (h s) -> p h s", h=NH),
                                     expseg[:].rearrange("p (h s) -> p h s", h=NH),
                                     scoresT[:, 256 * z:256 * (z + 1)].unsqueeze(1)
                                     .broadcast_to([128, NH, 256]),
                                     ALU.mult).then_inc(EV, 1)
            if c >= 1:
                vector.wait_ge(GS, c0["GS"] + 16 * c)
            for pr in range(16):
                vector.wait_ge(MM, mm0(c) + MMO_CB0 + (pr + 1))
                h0 = 2 * pr
                rb, cb = hCS(h0)
                vector.tensor_tensor(Cs[rb:rb + 64, cb:cb + 512].rearrange("p (h s) -> p h s", h=2),
                                     psCB[pr % 2][0:64, :].rearrange("p (h s) -> p h s", h=2),
                                     c_bf[p][:].unsqueeze(1).broadcast_to([64, 2, 256]),
                                     ALU.mult).then_inc(EV, 1)
            vector.wait_ge(EV, base + EVO["dbc"])
            for h in range(NH):
                vector.wait_ge(MM, mm0(c) + MMO_H0 + 2 * h + 1)
                if h >= 16:
                    hg2 = h // 8 - 2
                    vector.wait_ge(GY[hg2 % 2], c0[f"GY{hg2 % 2}"] + 16 * (2 * c + 1))
                elif h < 16 and h % 8 == 0 and c >= 1:
                    vector.wait_ge(GY[(h // 8) % 2], c0[f"GY{(h // 8) % 2}"] + 16 * 2 * c)
                slot = h % 2
                vector.scalar_tensor_tensor(
                    y_stage[(h // 8) % 2][:, 256 * (h % 8):256 * (h % 8 + 1)],
                    xT[:, 256 * h:256 * (h + 1)], P["d_bc"][:, h:h + 1],
                    psY[slot][:, 0:256],
                    ALU.mult, ALU.add).then_inc(EV, 1)
                vector.wait_ge(MM, mm0(c) + MMO_H0 + 2 * h + 2)
                rb, cb = hP(h)
                sbank = psS if slot == 0 else psCB[1]
                vector.scalar_tensor_tensor(
                    PP[(c + 1) % 2][rb:rb + 64, cb:cb + 128],
                    PP[c % 2][rb:rb + 64, cb:cb + 128],
                    dec_bc[rb:rb + 64, h:h + 1],
                    sbank[0:64, 0:128],
                    ALU.mult, ALU.add).then_inc(EV, 1)

    def scalar_fn(scalar):
        for c in range(NCK):
            base = es0(c)
            scalar.wait_ge(EV, ev0(c) + EVO["cl"])
            if c >= 1:
                scalar.wait_ge(MM, mm_tr_done(c - 1))
            scalar.activation(decay_hm[:], cum[:], AF.Exp, bias=cumlast[:],
                              scale=-1.0).then_inc(ES, 1)
            if c >= 1:
                scalar.wait_ge(MM, mm_cs_done(c - 1))
            scalar.activation(expcum_hm[:], cum[:], AF.Exp).then_inc(ES, 1)
            scalar.wait_ge(GL, c0["GL"] + 16 * (c + 1))
            scalar.activation(decrow[:], clrow[:], AF.Exp).then_inc(ES, 1)
            for z in range(2):
                if z == 1:
                    scalar.wait_ge(EV, ev0(c) + EVO_M0 + 1)
                for j in range(16):
                    t = 16 * z + j
                    scalar.wait_ge(MM, mm0(c) + MMO_SEG0 + t + 1)
                    scalar.activation(expseg[:, 512 * j:512 * (j + 1)],
                                      psSEG[t % 2][:], AF.Exp).then_inc(ES, 1)

    def tensor_fn(tensor):
        for c in range(NCK):
            p = c % 2
            tensor.wait_ge(AD[p], mkld(c))
            if c >= 1:
                tensor.wait_ge(EV, ev0(c - 1) + EVO_CS0 + 16)
            for z in range(2):
                tensor.matmul(psSC[:, 256 * z:256 * (z + 1)], b_bf[p][:, 128 * z:128 * (z + 1)],
                              c_bf[p][:], start=True,
                              stop=True).then_inc(MM, 1 if z == 1 else 0) if z == 1 else                     tensor.matmul(psSC[:, 256 * z:256 * (z + 1)], b_bf[p][:, 128 * z:128 * (z + 1)],
                                  c_bf[p][:], start=True, stop=True)
            if c >= 1:
                tensor.wait_ge(EV, ev0(c - 1) + EVO_Y0 + 64)
            tensor.wait_ge(ES, es0(c) + ESO["dcr"])
            tensor.matmul(psY[0][0:128, 0:32], P["onesf_1x128"][:], decrow[:],
                          start=True, stop=True).then_inc(MM, 1)
            if c >= 1:
                tensor.wait_ge(EV, ev0(c - 1) + EVO["dtm1"])
            for z in range(2):
                if z == 1:
                    tensor.wait_ge(EV, ev0(c) + EVO["btm0"])
                tensor.transpose(psTR[0:128, 64 * z:64 * z + 64], b_bf[p][:, 128 * z:128 * (z + 1)],
                                 P["ident128"][0:64, 0:64]).then_inc(MM, 1)
            tensor.wait_ge(ES, es0(c) + ESO["dhm"])
            for z in range(2):
                tensor.wait_ge(EV, ev0(c) + EVO["btm1" if z == 0 else "dtm0"])
                tensor.transpose(psTR[0:128, 128 + 32 * z:160 + 32 * z],
                                 decay_hm[:, 128 * z:128 * (z + 1)],
                                 P["ident128"][0:32, 0:32]).then_inc(MM, 1)
            tensor.wait_ge(GR, c0["GR"] + 16 * (c + 1))
            tensor.wait_ge(EV, ev0(c) + EVO["nl"])
            for z in range(2):
                for j in range(16):
                    t = 16 * z + j
                    if t >= 2:
                        tensor.wait_ge(ES, es0(c) + ESO_EXP0 + (t - 1))
                    tensor.matmul(psSEG[t % 2][:], negcum[0:66, 128 * z:128 * (z + 1)],
                                  rhs_seg[0:66, 512 * j:512 * (j + 1)], start=True, stop=False)
                    tensor.matmul(psSEG[t % 2][:], maskU[z][:],
                                  maskV[z][:], start=False, stop=True).then_inc(MM, 1)
            tensor.wait_ge(GC, c0["GC"] + 16 * (c + 1))
            tensor.wait_ge(EV, ev0(c) + EVO["sct"])
            if c >= 1:
                tensor.wait_ge(EV, ev0(c - 1) + EVO_Y0 + 64)
            for pr in range(16):
                if pr >= 2:
                    tensor.wait_ge(EV, ev0(c) + EVO_CS0 + (pr - 1))
                tensor.matmul(psCB[pr % 2][0:64, :],
                              P["onesb_1x64"][:], expc_row[0:1, 512 * pr:512 * (pr + 1)],
                              start=True, stop=True).then_inc(MM, 1)
            tensor.wait_ge(EV, ev0(c) + EVO_BD0 + 64)
            tensor.wait_ge(EV, ev0(c) + EVO_M0 + 2)
            tensor.wait_ge(EV, ev0(c) + EVO["pbf"])
            tensor.wait_ge(EV, ev0(c) + EVO_CS0 + 16)
            for h in range(NH):
                slot = h % 2
                if h >= 2:
                    tensor.wait_ge(EV, ev0(c) + EVO_Y0 + 2 * (h - 2) + 2)
                elif c >= 1:
                    tensor.wait_ge(EV, ev0(c - 1) + EVO_Y0 + 2 * (h + 30) + 2)
                if h < 2:
                    tensor.wait_ge(EV, ev0(c) + EVO["dbc"])
                rb, cb = hCS(h)
                mm1 = tensor.matmul(psY[slot][:, 0:256], xdt[0][:, 128 * h:128 * (h + 1)],
                                    M[0][:, 256 * h:256 * (h + 1)], start=True, stop=False)
                last = tensor.matmul(psY[slot][:, 0:256], xdt[1][:, 128 * h:128 * (h + 1)],
                                     M[1][:, 256 * h:256 * (h + 1)], start=False, stop=(c == 0))
                if c > 0:
                    prb, pcb = hP(h)
                    last = tensor.matmul(psY[slot][:, 0:256],
                                         P_bf[prb:prb + 64, pcb:pcb + 128],
                                         Cs[rb:rb + 64, cb:cb + 256], start=False, stop=True)
                last.then_inc(MM, 1)
                sbank = psS if slot == 0 else psCB[1]
                sl = tensor.matmul(sbank[0:64, 0:128],
                                   Bdec[0][:, 64 * h:64 * (h + 1)],
                                   xdt[0][:, 128 * h:128 * (h + 1)], start=True, stop=False)
                sl = tensor.matmul(sbank[0:64, 0:128],
                                   Bdec[1][:, 64 * h:64 * (h + 1)],
                                   xdt[1][:, 128 * h:128 * (h + 1)], start=False, stop=True)
                sl.then_inc(MM, 1)

    ctx.cnt["LD"] = c0["LD"] + PRE_LD
    ctx.cnt["AD0"] = c0["AD0"] + LDC * (NCK // 2)
    ctx.cnt["AD1"] = c0["AD1"] + LDC * (NCK // 2)
    ctx.cnt["GA"] = c0["GA"] + 32 * NCK
    for n in ("GB", "GE", "GR", "GL", "GC", "GS"):
        ctx.cnt[n] = c0[n] + 16 * NCK
    ctx.cnt["GY0"] = c0["GY0"] + 32 * NCK
    ctx.cnt["GY1"] = c0["GY1"] + 32 * NCK
    ctx.cnt["LG"] = c0["LG"] + 5 * 16
    ctx.cnt["EV"] = c0["EV"] + PRE_EV + EVC * NCK
    ctx.cnt["ES"] = c0["ES"] + ESC * NCK
    ctx.cnt["MM"] = c0["MM"] + MMC * NCK
    return {"sync": sync_fn, "tensor": tensor_fn, "scalar": scalar_fn,
            "vector": vector_fn, "gpsimd": gpsimd_fn}


def phase_X(ctx):
    nc, st, P = ctx.nc, ctx.pstack, ctx.P
    inp, d = ctx.inp, ctx.d
    LD, LG, MM, EV, ES, CC = (ctx.sem(n) for n in ("LD", "LG", "MM", "EV", "ES", "CC"))
    c0 = ctx.snap()

    if os.environ.get("X_ONLY_CC") == "1":
        def gp_only(gpsimd):
            gpsimd.collective_compute(
                "AllGather", ALU.bypass,
                replica_groups=[[0, 1, 2, 3, 4, 5, 6, 7]],
                ins=[d["exch_in"].opt()],
                outs=[d["exch_out"].opt()],
            ).then_inc(CC, 1)
        ctx.cnt["CC"] = c0["CC"] + 1
        return {"gpsimd": gp_only}

    Sj = st.enter_context(nc.sbuf_tensor("x_Sj", [128, 2048], F32))
    totr = st.enter_context(nc.sbuf_tensor("x_totr", [8, 32], F32))
    wv = st.enter_context(nc.sbuf_tensor("x_wv", [8, 32], F32))
    wrow = st.enter_context(nc.sbuf_tensor("x_wrow", [1, 32], F32))
    wbc = st.enter_context(nc.sbuf_tensor("x_wbc", [128, 32], F32))
    Pacc = [st.enter_context(nc.sbuf_tensor(f"x_Pa{i}", [128, 2048], F32)) for i in range(2)]
    cdpr = st.enter_context(nc.sbuf_tensor("x_cdpr", [16, 32], F32))
    cdpe = st.enter_context(nc.sbuf_tensor("x_cdpe", [16, 32], F32))
    crow = st.enter_context(nc.sbuf_tensor("x_crow", [1, 32], F32))
    px = st.enter_context(nc.psum_tensor("x_px", [128, 512], F32))

    def gpsimd_fn(gpsimd):
        gpsimd.collective_compute(
            "AllGather", ALU.bypass,
            replica_groups=[[0, 1, 2, 3, 4, 5, 6, 7]],
            ins=[d["exch_in"].opt()],
            outs=[d["exch_out"].opt()],
        ).then_inc(CC, 1)
        gpsimd.wait_ge(CC, c0["CC"] + 1)
        gpsimd.wait_ge(ES, c0["ES"] + 1)
        gpsimd.dma_start(out=d["cl_sp"][0:8, :], in_=wv[:]).then_inc(LG, 16)
        gpsimd.wait_ge(ES, c0["ES"] + 2)
        gpsimd.dma_start(out=d["cdp_sp"][:, :], in_=cdpe[:]).then_inc(LG, 16)
        gpsimd.wait_ge(EV, c0["EV"] + 281)
        gpsimd.dma_start(out=d["dbg_sp"][0, 0:128, :], in_=P["p_init"][:, 0:512]).then_inc(LG, 16)
        gpsimd.dma_start(out=d["dbg_sp"][1, 0:128, :], in_=P["cdp_bc"][:]).then_inc(LG, 16)
        gpsimd.dma_start(out=d["dbg_sp"][2, 0:8, 0:32], in_=wv[:]).then_inc(LG, 16)
        gpsimd.dma_start(out=d["dbg_sp"][3, 0:128, 0:32], in_=wbc[:]).then_inc(LG, 16)
        gpsimd.dma_start(out=d["dbg_sp"][7, 0:16, 0:32], in_=cdpe[:]).then_inc(LG, 16)
        gpsimd.dma_start(out=d["dbg_sp"][7, 0:8, 64:96], in_=totr[:]).then_inc(LG, 16)
        gpsimd.dma_start(out=d["dbg_sp"][7, 0:16, 128:160], in_=cdpr[:]).then_inc(LG, 16)

    def sync_fn2(sync):
        sync.wait_ge(CC, c0["CC"] + 1)
        for j in range(8):
            sync.dma_start(out=totr[j:j + 1, :],
                           in_=d["exch_out"][130 * j + 128:130 * j + 129, 0:32]).then_inc(LD, 16)
        sync.dma_start(out=cdpr[:], in_=d["cdp_sp"][:, :]).then_inc(LD, 16)
        for j in range(8):
            if j >= 1:
                sync.wait_ge(EV, c0["EV"] + 33 * j)
            sync.dma_start(out=Sj[:], in_=d["exch_out"][130 * j:130 * j + 128, :]
                           ).then_inc(LD, 16)
            if j == 0:
                sync.wait_ge(LG, c0["LG"] + 16)
            else:
                sync.wait_ge(MM, c0["MM"] + 1 + j)
            sync.dma_start(out=wrow[:], in_=d["cl_sp"][j:j + 1, :]).then_inc(LD, 16)
        sync.wait_ge(LG, c0["LG"] + 32)
        for cc_ in range(16):
            if cc_ >= 1:
                sync.wait_ge(MM, c0["MM"] + 9 + cc_)
            sync.dma_start(out=crow[:], in_=d["cdp_sp"][cc_:cc_ + 1, :]).then_inc(LD, 16)

    def tensor_fn(tensor):
        tensor.wait_ge(LD, c0["LD"] + 16 * 9)
        tensor.matmul(px[0:8, 0:32], P["combA"][:], totr[:], start=True, stop=True).then_inc(MM, 1)
        tensor.wait_ge(ES, c0["ES"] + 1)
        for j in range(8):
            tensor.wait_ge(LD, c0["LD"] + 16 * (11 + 2 * j))
            if j >= 1:
                tensor.wait_ge(EV, c0["EV"] + 33 * (j - 1) + 1)
            tensor.matmul(px[0:128, 32:64], P["onesf_1x128"][:], wrow[:],
                          start=True, stop=True).then_inc(MM, 1)
        tensor.wait_ge(EV, c0["EV"] + 265)
        for cc_ in range(16):
            tensor.wait_ge(LD, c0["LD"] + 16 * (9 + 8 + 8 + cc_ + 1))
            if cc_ >= 1:
                tensor.wait_ge(EV, c0["EV"] + 265 + cc_)
            tensor.matmul(px[0:128, 64 + 32 * (cc_ % 4):96 + 32 * (cc_ % 4)],
                          P["onesf_1x128"][:], crow[:], start=True, stop=True).then_inc(MM, 1)

    def scalar_fn(scalar):
        scalar.wait_ge(MM, c0["MM"] + 1)
        scalar.activation(wv[:], px[0:8, 0:32], AF.Exp, bias=P["combB"][:]).then_inc(ES, 1)
        scalar.wait_ge(LD, c0["LD"] + 16 * 9)
        scalar.activation(cdpe[:], cdpr[:], AF.Exp).then_inc(ES, 1)

    def vector_fn(vector):
        for j in range(8):
            vector.wait_ge(MM, c0["MM"] + 1 + (j + 1))
            vector.tensor_copy(wbc[:], px[0:128, 32:64]).then_inc(EV, 1)
            vector.wait_ge(EV, c0["EV"] + 33 * j + 1)
            vector.wait_ge(LD, c0["LD"] + 16 * (10 + 2 * j))
            for h in range(NH):
                rb, cb = hP(h)
                if j == 0:
                    vector.tensor_scalar(Pacc[0][rb:rb + 64, cb:cb + 128],
                                         Sj[rb:rb + 64, cb:cb + 128],
                                         wbc[rb:rb + 64, h:h + 1], None,
                                         ALU.mult).then_inc(EV, 1)
                else:
                    vector.scalar_tensor_tensor(Pacc[j % 2][rb:rb + 64, cb:cb + 128],
                                                Sj[rb:rb + 64, cb:cb + 128],
                                                wbc[rb:rb + 64, h:h + 1],
                                                Pacc[(j + 1) % 2][rb:rb + 64, cb:cb + 128],
                                                ALU.mult, ALU.add).then_inc(EV, 1)
        vector.tensor_copy(P["p_init"][:], Pacc[1][:]).then_inc(EV, 1)
        for cc_ in range(16):
            vector.wait_ge(MM, c0["MM"] + 9 + cc_ + 1)
            vector.wait_ge(EV, c0["EV"] + 265 + cc_)
            vector.tensor_copy(P["cdp_bc"][:, 32 * cc_:32 * (cc_ + 1)],
                               px[0:128, 64 + 32 * (cc_ % 4):96 + 32 * (cc_ % 4)]).then_inc(EV, 1)

    ctx.cnt["CC"] = c0["CC"] + 1
    ctx.cnt["LG"] = c0["LG"] + 32 + 112
    ctx.cnt["LD"] = c0["LD"] + 16 * (9 + 8 + 8 + 16)
    ctx.cnt["MM"] = c0["MM"] + 25
    ctx.cnt["ES"] = c0["ES"] + 2
    ctx.cnt["EV"] = c0["EV"] + 281
    return {"sync": sync_fn2, "tensor": tensor_fn, "scalar": scalar_fn,
            "vector": vector_fn, "gpsimd": gpsimd_fn}


def phase_B1(ctx):
    nc, st, P = ctx.nc, ctx.pstack, ctx.P
    inp, d = ctx.inp, ctx.d
    MM, EV, ES = (ctx.sem(n) for n in ("MM", "EV", "ES"))
    BD = [ctx.sem("BD0"), ctx.sem("BD1")]
    GN = [ctx.sem("GN0"), ctx.sem("GN1")]
    c0 = ctx.snap()

    y_c = [st.enter_context(nc.sbuf_tensor(f"b1_y{p}", [128, HS], BF16)) for p in range(2)]
    g_c = [st.enter_context(nc.sbuf_tensor(f"b1_g{p}", [128, HS], BF16)) for p in range(2)]
    cs_c = [st.enter_context(nc.sbuf_tensor(f"b1_cs{p}", [128, 2 * DM], BF16)) for p in range(2)]
    Pc = st.enter_context(nc.sbuf_tensor("b1_Pc", [128, 2048], F32))
    Pc_bf = st.enter_context(nc.sbuf_tensor("b1_Pcb", [128, 2048], BF16))
    yg_s = st.enter_context(nc.sbuf_tensor("b1_yg", [128, HS], BF16))
    sq_bf = st.enter_context(nc.sbuf_tensor("b1_sq", [128, HS], BF16))
    yn_s = [st.enter_context(nc.sbuf_tensor(f"b1_yn{p}", [128, HS], BF16)) for p in range(2)]
    lnv = st.enter_context(nc.sbuf_tensor("b1_lnv", [1, 256], F32))
    rstd_row = st.enter_context(nc.sbuf_tensor("b1_rrow", [1, 256], F32))
    rstd_bc = st.enter_context(nc.sbuf_tensor("b1_rbc", [128, 256], F32))
    psB = [st.enter_context(nc.psum_tensor(f"b1_psB{i}", [128, 512], F32)) for i in range(2)]
    psR = st.enter_context(nc.psum_tensor("b1_psR", [128, 512], F32))

    EVC = 32 + 1 + 16 + 1 + 1
    ESC = 6
    MMC = 18

    def ev0(c):
        return c0["EV"] + EVC * c

    def es0(c):
        return c0["ES"] + ESC * c

    def mm0(c):
        return c0["MM"] + MMC * c

    def sync_fn(sync):
        for c in range(NCK):
            p = c % 2
            t0 = CTX + 256 * c
            if c >= 2:
                sync.wait_ge(EV, ev0(c - 1))
            sync.dma_start(out=y_c[p][:], in_=d["y_sp"][c]).then_inc(BD[p], 16)
            sync.dma_start(out=g_c[p][:].rearrange("p (t c2) -> p t c2", t=32),
                           in_=d["g_sp"][0:DI, t0:t0 + 256].rearrange("(t p) c2 -> p t c2", p=128)
                           ).then_inc(BD[p], 16)
            sync.dma_start(out=cs_c[p][:], in_=d["cs_sp"][c]).then_inc(BD[p], 16)

    def vector_fn(vector):
        for c in range(NCK):
            p = c % 2
            base = ev0(c)
            if c >= 1:
                vector.wait_ge(MM, mm0(c - 1) + 16)
            for h in range(NH):
                rb, cb = hP(h)
                vector.tensor_scalar(Pc[rb:rb + 64, cb:cb + 128],
                                     P["p_init"][rb:rb + 64, cb:cb + 128],
                                     P["cdp_bc"][rb:rb + 64, 32 * c + h:32 * c + h + 1],
                                     None, ALU.mult).then_inc(EV, 1)
            vector.wait_ge(EV, base + 32)
            vector.tensor_copy(Pc_bf[:], Pc[:]).then_inc(EV, 1)
            for pr in range(16):
                vector.wait_ge(MM, mm0(c) + pr + 1)
                vector.tensor_mul(yg_s[:, 512 * pr:512 * (pr + 1)], psB[pr % 2][:],
                                  g_c[p][:, 512 * pr:512 * (pr + 1)]).then_inc(EV, 1)
            vector.wait_ge(MM, mm0(c) + 18)
            vector.tensor_copy(rstd_bc[:], psR[0:128, 256:512]).then_inc(EV, 1)
            vector.wait_ge(EV, base + 32 + 1 + 16 + 1)
            if c >= 2:
                vector.wait_ge(GN[p], c0[f"GN{p}"] + 32 * (c // 2) + (48 if p == 0 else 0))
            vector.tensor_tensor(yn_s[p][:].rearrange("q (h s2) -> q h s2", h=NH),
                                 yg_s[:].rearrange("q (h s2) -> q h s2", h=NH),
                                 rstd_bc[:].unsqueeze(1).broadcast_to([128, NH, 256]),
                                 ALU.mult).then_inc(EV, 1)

    def tensor_fn(tensor):
        for c in range(NCK):
            p = c % 2
            tensor.wait_ge(BD[p], c0[f"BD{p}"] + 48 * (c // 2 + 1))
            tensor.wait_ge(EV, ev0(c) + 33)
            for pr in range(16):
                if pr >= 2:
                    tensor.wait_ge(EV, ev0(c) + 33 + (pr - 1))
                elif c >= 1:
                    tensor.wait_ge(EV, ev0(c - 1) + 33 + 16)
                last = tensor.matmul(psB[pr % 2][:], P["ident128"][:],
                                     y_c[p][:, 512 * pr:512 * (pr + 1)], start=True, stop=False)
                for hh in range(2):
                    h = 2 * pr + hh
                    prb, pcb = hP(h)
                    rb, cb = hCS(h)
                    last = tensor.matmul(psB[pr % 2][:, 256 * hh:256 * (hh + 1)],
                                         Pc_bf[prb:prb + 64, pcb:pcb + 128],
                                         cs_c[p][rb:rb + 64, cb:cb + 256],
                                         start=False, stop=(hh == 1))
                last.then_inc(MM, 1)
            for h in range(NH):
                tensor.wait_ge(ES, es0(c) + 1 + h // 8)
                if h == 0 and c >= 1:
                    tensor.wait_ge(EV, ev0(c - 1) + EVC)
                mmi = tensor.matmul(psR[0:1, 0:256], P["onesb_128x1"][:],
                                    sq_bf[:, 256 * h:256 * (h + 1)],
                                    start=(h == 0), stop=(h == NH - 1))
            mmi.then_inc(MM, 1)
            tensor.wait_ge(ES, es0(c) + 6)
            tensor.matmul(psR[0:128, 256:512][:, 0:256], P["onesf_1x128"][:],
                          rstd_row[:], start=True, stop=True).then_inc(MM, 1)

    def scalar_fn(scalar):
        for c in range(NCK):
            for q in range(4):
                scalar.wait_ge(EV, ev0(c) + 33 + 4 * (q + 1))
                scalar.activation(sq_bf[:, 2048 * q:2048 * (q + 1)],
                                  yg_s[:, 2048 * q:2048 * (q + 1)],
                                  AF.Square).then_inc(ES, 1)
            scalar.wait_ge(MM, mm0(c) + 17)
            scalar.activation(lnv[:], psR[0:1, 0:256], AF.Ln, scale=1.0 / DI,
                              bias=P["eps_col"][0:1, :]).then_inc(ES, 1)
            scalar.wait_ge(ES, es0(c) + 5)
            scalar.activation(rstd_row[:], lnv[:], AF.Exp, scale=-0.5).then_inc(ES, 1)

    def gpsimd_fn(gpsimd):
        for c in range(NCK):
            p = c % 2
            gpsimd.wait_ge(EV, ev0(c) + EVC)
            for th in range(2):
                gpsimd.dma_start(
                    out=d["yn_sp"][c][:, 4096 * th:4096 * (th + 1)]
                    .rearrange("q (h s) -> q h s", h=32),
                    in_=yn_s[p][:].rearrange("q (h ts) -> q h ts", h=32)[:, :, 128 * th:128 * (th + 1)],
                ).then_inc(GN[p], 16)
            if c == 0:
                gpsimd.dma_start(out=d["dbg_sp"][4, :, :], in_=yg_s[:, 0:512]).then_inc(GN[p], 16)
                gpsimd.dma_start(out=d["dbg_sp"][5, :, :], in_=sq_bf[:, 0:512]).then_inc(GN[p], 16)
                gpsimd.dma_start(out=d["dbg_sp"][6, :, 0:256], in_=rstd_bc[:]).then_inc(GN[p], 16)

    ctx.cnt["EV"] = c0["EV"] + EVC * NCK
    ctx.cnt["ES"] = c0["ES"] + ESC * NCK
    ctx.cnt["MM"] = c0["MM"] + MMC * NCK
    ctx.cnt["BD0"] = c0["BD0"] + 48 * (NCK // 2)
    ctx.cnt["BD1"] = c0["BD1"] + 48 * (NCK // 2)
    ctx.cnt["GN0"] = c0["GN0"] + 32 * (NCK // 2) + 48
    ctx.cnt["GN1"] = c0["GN1"] + 32 * (NCK // 2)
    return {"sync": sync_fn, "tensor": tensor_fn, "scalar": scalar_fn,
            "vector": vector_fn, "gpsimd": gpsimd_fn}


def phase_B2(ctx):
    nc, st, P = ctx.nc, ctx.pstack, ctx.P
    inp, d = ctx.inp, ctx.d
    MM, EV, ES = (ctx.sem(n) for n in ("MM", "EV", "ES"))
    BW = ctx.sem("BW")
    BY = [ctx.sem("BY0"), ctx.sem("BY1")]
    BX = [ctx.sem("BX0"), ctx.sem("BX1")]
    c0 = ctx.snap()

    WoT = [st.enter_context(nc.sbuf_tensor(f"b2_W{e}", [128, DM], BF16)) for e in range(32)]
    yn_c = [st.enter_context(nc.sbuf_tensor(f"b2_yn{p}", [128, HS // 2], BF16)) for p in range(2)]
    u_c = [st.enter_context(nc.sbuf_tensor(f"b2_u{p}", [128, DM], F32)) for p in range(2)]
    x1st = st.enter_context(nc.sbuf_tensor("b2_x1", [128, DM], F32))
    x1o = [st.enter_context(nc.sbuf_tensor(f"b2_x1o{p}", [128, DM], F32)) for p in range(2)]
    mu = st.enter_context(nc.sbuf_tensor("b2_mu", [128, 1], F32))
    m2 = st.enter_context(nc.sbuf_tensor("b2_m2", [128, 1], F32))
    ssq = st.enter_context(nc.sbuf_tensor("b2_ssq", [128, 1], F32))
    var = st.enter_context(nc.sbuf_tensor("b2_var", [128, 1], F32))
    lnv = st.enter_context(nc.sbuf_tensor("b2_lnv", [128, 1], F32))
    rstd = st.enter_context(nc.sbuf_tensor("b2_rstd", [128, 1], F32))
    psW = [st.enter_context(nc.psum_tensor(f"b2_ps{i}", [128, 512], F32)) for i in range(4)]

    EVC, ESC, MMC = 4 + 5 + 1, 1 + 2, 1

    def it0(name, c2):
        return c0[name] + c2

    def sync_fn(sync):
        for e in range(32):
            sync.dma_start(out=WoT[e][:], in_=d["wo_sp"][:, 128 * e:128 * (e + 1)],
                           transpose=True).then_inc(BW, 16)
        for c in range(NCK):
            p = c % 2
            for th in range(2):
                if c >= 1:
                    sync.wait_ge(MM, c0["MM"] + 2 * (c - 1) + th + 1)
                sync.dma_start(out=yn_c[th][:],
                               in_=d["yn_sp"][c][:, 4096 * th:4096 * (th + 1)]).then_inc(BY[p], 16)
            for th in range(2):
                if c >= 1:
                    sync.wait_ge(ES, c0["ES"] + 3 * (2 * (c - 1) + th) + 1)
                sync.dma_start(out=u_c[th][:],
                               in_=inp["u_pad"][CTX + 256 * c + 128 * th:
                                                CTX + 256 * c + 128 * (th + 1), :]
                               ).then_inc(BY[p], 16)

    def tensor_fn(tensor):
        tensor.wait_ge(BW, c0["BW"] + 16 * 32)
        it = 0
        for c in range(NCK):
            p = c % 2
            tensor.wait_ge(BY[p], c0[f"BY{p}"] + 64 * (c // 2 + 1))
            for th in range(2):
                if it >= 1:
                    tensor.wait_ge(EV, c0["EV"] + 10 * it - 6)
                for e in range(32):
                    for dg in range(4):
                        mmi = tensor.matmul(psW[dg][:], yn_c[th][:, 128 * e:128 * (e + 1)],
                                            WoT[e][:, 512 * dg:512 * (dg + 1)],
                                            start=(e == 0), stop=(e == 31))
                mmi.then_inc(MM, 1)
                it += 1

    def vector_fn(vector):
        it = 0
        for c in range(NCK):
            p = c % 2
            for th in range(2):
                base = c0["EV"] + 10 * it
                vector.wait_ge(MM, c0["MM"] + it + 1)
                if it >= 1:
                    vector.wait_ge(ES, c0["ES"] + 3 * (it - 1) + 1)
                for dg in range(4):
                    vector.tensor_add(x1st[:, 512 * dg:512 * (dg + 1)], psW[dg][:],
                                      u_c[th][:, 512 * dg:512 * (dg + 1)]
                                      ).then_inc(EV, 1)
                vector.wait_ge(EV, base + 4)
                vector.tensor_reduce(mu[:], x1st[:], AX.X, ALU.add).then_inc(EV, 1)
                vector.wait_ge(EV, base + 5)
                vector.tensor_scalar(mu[:], mu[:], 1.0 / DM, None, ALU.mult).then_inc(EV, 1)
                vector.wait_ge(EV, base + 6)
                vector.tensor_mul(m2[:], mu[:], mu[:]).then_inc(EV, 1)
                vector.wait_ge(ES, c0["ES"] + 3 * it + 1)
                vector.wait_ge(EV, base + 7)
                vector.tensor_scalar(var[:], ssq[:], 1.0 / DM, m2[:], ALU.mult,
                                     ALU.subtract).then_inc(EV, 1)
                vector.wait_ge(ES, c0["ES"] + 3 * it + 3)
                if it >= 2:
                    vector.wait_ge(BX[it % 2], c0[f"BX{it % 2}"] + 32 * (it // 2))
                vector.wait_ge(EV, base + 8)
                vector.tensor_scalar(x1o[it % 2][:], x1st[:], mu[:], rstd[:],
                                     ALU.subtract, ALU.mult).then_inc(EV, 1)
                vector.wait_ge(EV, base + 9)
                vector.nop().then_inc(EV, 1)
                it += 1

    def scalar_fn(scalar):
        it = 0
        for c in range(NCK):
            for th in range(2):
                base = c0["EV"] + 10 * it
                scalar.wait_ge(EV, base + 5)
                scalar.activation(u_c[th][:], x1st[:], AF.Square,
                                  accum_out=ssq[:]).then_inc(ES, 1)
                scalar.wait_ge(EV, base + 8)
                scalar.activation(lnv[:], var[:], AF.Ln, bias=P["eps_col"][:]).then_inc(ES, 1)
                scalar.wait_ge(ES, c0["ES"] + 3 * it + 2)
                scalar.activation(rstd[:], lnv[:], AF.Exp, scale=-0.5).then_inc(ES, 1)
                it += 1

    def gpsimd_fn(gpsimd):
        it = 0
        for c in range(NCK):
            for th in range(2):
                gpsimd.wait_ge(EV, c0["EV"] + 10 * it + 9)
                r0 = 256 * c + 128 * th
                gpsimd.dma_start(out=d["x1f_sp"][r0:r0 + 128, :],
                                 in_=x1o[it % 2][:]).then_inc(BX[it % 2], 16)
                gpsimd.dma_start(out=d["x1b_sp"][r0:r0 + 128, :],
                                 in_=x1o[it % 2][:]).then_inc(BX[it % 2], 16)
                it += 1

    ctx.cnt["BW"] = c0["BW"] + 16 * 32
    ctx.cnt["BY0"] = c0["BY0"] + 64 * (NCK // 2)
    ctx.cnt["BY1"] = c0["BY1"] + 64 * (NCK // 2)
    ctx.cnt["BX0"] = c0["BX0"] + 32 * NCK // 2 * 2
    ctx.cnt["BX1"] = c0["BX1"] + 32 * NCK // 2 * 2
    ctx.cnt["MM"] = c0["MM"] + 2 * NCK
    ctx.cnt["EV"] = c0["EV"] + 10 * 2 * NCK
    ctx.cnt["ES"] = c0["ES"] + 3 * 2 * NCK
    return {"sync": sync_fn, "tensor": tensor_fn, "scalar": scalar_fn,
            "vector": vector_fn, "gpsimd": gpsimd_fn}


def phase_B3(ctx):
    nc, st, P = ctx.nc, ctx.pstack, ctx.P
    inp, d = ctx.inp, ctx.d
    MM, EV, ES, LD = (ctx.sem(n) for n in ("MM", "EV", "ES", "LD"))
    BQ = ctx.sem("BQ")
    BT = [ctx.sem("BT0"), ctx.sem("BT1")]
    BF_ = [ctx.sem("BF0"), ctx.sem("BF1")]
    OUT = ctx.sem("OUT")
    c0 = ctx.snap()

    qT = [st.enter_context(nc.sbuf_tensor(f"b3_q{i}", [128, DM], BF16)) for i in range(16)]
    x1T = [[st.enter_context(nc.sbuf_tensor(f"b3_xt{g}_{i}", [128, 1024], BF16))
            for i in range(16)] for g in range(2)]
    x1f = [st.enter_context(nc.sbuf_tensor(f"b3_x1f{p}", [128, 2 * DM], F32)) for p in range(2)]
    outs = st.enter_context(nc.sbuf_tensor("b3_outs", [128, DM], F32))
    outo = [st.enter_context(nc.sbuf_tensor(f"b3_oo{p}", [128, DM], F32)) for p in range(2)]
    sqs = st.enter_context(nc.sbuf_tensor("b3_sqs", [128, DM], BF16))
    mu = st.enter_context(nc.sbuf_tensor("b3_mu", [128, 1], F32))
    m2 = st.enter_context(nc.sbuf_tensor("b3_m2", [128, 1], F32))
    ssq = st.enter_context(nc.sbuf_tensor("b3_ssq", [128, 1], F32))
    var = st.enter_context(nc.sbuf_tensor("b3_var", [128, 1], F32))
    lnv = st.enter_context(nc.sbuf_tensor("b3_lnv", [128, 1], F32))
    rstd = st.enter_context(nc.sbuf_tensor("b3_rstd", [128, 1], F32))
    psF = [st.enter_context(nc.psum_tensor(f"b3_ps{i}", [128, 512], F32)) for i in range(4)]

    def sync_fn(sync):
        for i in range(16):
            sync.dma_start(out=qT[i][:], in_=d["q_sp"][:, 128 * i:128 * (i + 1)],
                           transpose=True).then_inc(BQ, 16)
        def load_x1t(g):
            gp = g % 2
            for i in range(16):
                sync.dma_start(out=x1T[gp][i][:],
                               in_=d["x1b_sp"][1024 * g:1024 * (g + 1), 128 * i:128 * (i + 1)],
                               transpose=True).then_inc(BT[gp], 16)
        for g in range(2):
            load_x1t(g)
        for c in range(NCK):
            p = c % 2
            if c == 6:
                sync.wait_ge(MM, c0["MM"] + 8)
                load_x1t(2)
            if c == 10:
                sync.wait_ge(MM, c0["MM"] + 16)
                load_x1t(3)
            if c >= 2:
                sync.wait_ge(EV, c0["EV"] + 10 * (2 * c - 3) + 4)
            sync.dma_start(out=x1f[p][:].rearrange("q (t c2) -> q t c2", t=2),
                           in_=d["x1f_sp"][256 * c:256 * (c + 1), :]
                           .rearrange("(t q) c2 -> q t c2", q=128)).then_inc(BF_[p], 16)

    def tensor_fn(tensor):
        tensor.wait_ge(BQ, c0["BQ"] + 16 * 16)
        it = 0
        for c in range(NCK):
            g = c // 4
            gp = g % 2
            tensor.wait_ge(BT[gp], c0[f"BT{gp}"] + 256 * (g // 2 + 1))
            for th in range(2):
                if it >= 1:
                    tensor.wait_ge(EV, c0["EV"] + 10 * it - 6)
                loc = 256 * (c % 4) + 128 * th
                for i in range(16):
                    for og in range(4):
                        mmi = tensor.matmul(psF[og][:], x1T[gp][i][:, loc:loc + 128],
                                            qT[i][:, 512 * og:512 * (og + 1)],
                                            start=(i == 0), stop=False)
                for og in range(4):
                    mmi = tensor.matmul(psF[og][:], P["onesb_1x128"][:],
                                        P["bias_bc"][0:1, 512 * og:512 * (og + 1)],
                                        start=False, stop=True)
                mmi.then_inc(MM, 1)
                it += 1

    def vector_fn(vector):
        it = 0
        for c in range(NCK):
            p = c % 2
            for th in range(2):
                base = c0["EV"] + 10 * it
                vector.wait_ge(MM, c0["MM"] + it + 1)
                if it >= 1:
                    vector.wait_ge(ES, c0["ES"] + 3 * (it - 1) + 1)
                vector.wait_ge(BF_[p], c0[f"BF{p}"] + 16 * (c // 2 + 1))
                for dg in range(4):
                    vector.tensor_add(outs[:, 512 * dg:512 * (dg + 1)], psF[dg][:],
                                      x1f[p][:, 2048 * th + 512 * dg:2048 * th + 512 * (dg + 1)]
                                      ).then_inc(EV, 1)
                vector.wait_ge(EV, base + 4)
                vector.tensor_reduce(mu[:], outs[:], AX.X, ALU.add).then_inc(EV, 1)
                vector.wait_ge(EV, base + 5)
                vector.tensor_scalar(mu[:], mu[:], 1.0 / DM, None, ALU.mult).then_inc(EV, 1)
                vector.wait_ge(EV, base + 6)
                vector.tensor_mul(m2[:], mu[:], mu[:]).then_inc(EV, 1)
                vector.wait_ge(ES, c0["ES"] + 3 * it + 1)
                vector.wait_ge(EV, base + 7)
                vector.tensor_scalar(var[:], ssq[:], 1.0 / DM, m2[:], ALU.mult,
                                     ALU.subtract).then_inc(EV, 1)
                vector.wait_ge(ES, c0["ES"] + 3 * it + 3)
                if it >= 2:
                    vector.wait_ge(OUT, c0["OUT"] + 16 * (it - 1))
                vector.wait_ge(EV, base + 8)
                vector.tensor_scalar(outo[it % 2][:], outs[:], mu[:], rstd[:],
                                     ALU.subtract, ALU.mult).then_inc(EV, 1)
                vector.wait_ge(EV, base + 9)
                vector.nop().then_inc(EV, 1)
                it += 1

    def scalar_fn(scalar):
        it = 0
        for c in range(NCK):
            for th in range(2):
                base = c0["EV"] + 10 * it
                scalar.wait_ge(EV, base + 5)
                scalar.activation(sqs[:], outs[:], AF.Square,
                                  accum_out=ssq[:]).then_inc(ES, 1)
                scalar.wait_ge(EV, base + 8)
                scalar.activation(lnv[:], var[:], AF.Ln, bias=P["eps_col"][:]).then_inc(ES, 1)
                scalar.wait_ge(ES, c0["ES"] + 3 * it + 2)
                scalar.activation(rstd[:], lnv[:], AF.Exp, scale=-0.5).then_inc(ES, 1)
                it += 1

    def gpsimd_fn(gpsimd):
        it = 0
        for c in range(NCK):
            for th in range(2):
                gpsimd.wait_ge(EV, c0["EV"] + 10 * it + 9)
                r0 = 256 * c + 128 * th
                gpsimd.dma_start(out=ctx.out_ap[r0:r0 + 128, :],
                                 in_=outo[it % 2][:]).then_inc(OUT, 16)
                it += 1
        gpsimd.wait_ge(OUT, c0["OUT"] + 16 * 32)

    ctx.cnt["BQ"] = c0["BQ"] + 16 * 16
    ctx.cnt["BT0"] = c0["BT0"] + 16 * 32
    ctx.cnt["BT1"] = c0["BT1"] + 16 * 32
    ctx.cnt["BF0"] = c0["BF0"] + 16 * (NCK // 2)
    ctx.cnt["BF1"] = c0["BF1"] + 16 * (NCK // 2)
    ctx.cnt["OUT"] = c0["OUT"] + 16 * 32
    ctx.cnt["MM"] = c0["MM"] + 2 * NCK
    ctx.cnt["EV"] = c0["EV"] + 10 * 2 * NCK
    ctx.cnt["ES"] = c0["ES"] + 3 * 2 * NCK
    return {"sync": sync_fn, "tensor": tensor_fn, "scalar": scalar_fn,
            "vector": vector_fn, "gpsimd": gpsimd_fn}


def make_in_maps(inputs):
    u = np.ascontiguousarray(inputs["L0_attn_u"])
    consts = host_consts()
    maps = []
    for core in range(8):
        b, s = core // 4, core % 4
        lo = s * T - CTX
        if lo < 0:
            u_pad = np.concatenate([np.zeros((CTX, DM), np.float32),
                                    np.asarray(u[b, 0:(s + 1) * T], np.float32)], axis=0)
        else:
            u_pad = np.asarray(u[b, lo:(s + 1) * T], np.float32)
        m = dict(
            u_pad=np.ascontiguousarray(u_pad),
            w_in=np.ascontiguousarray(inputs["L0_attn_W_in"], dtype=np.float32),
            conv_w=np.ascontiguousarray(inputs["L0_attn_conv_w"], dtype=np.float32),
            conv_b=np.ascontiguousarray(np.asarray(inputs["L0_attn_conv_b"], dtype=np.float32).reshape(CONV_DIM, 1)),
            dt_bias=np.ascontiguousarray(np.asarray(inputs["L0_attn_dt_bias"], dtype=np.float32).reshape(NH, 1)),
            a_log=np.ascontiguousarray(np.asarray(inputs["L0_attn_A_log"], dtype=np.float32).reshape(NH, 1)),
            d_vec=np.ascontiguousarray(np.asarray(inputs["L0_attn_D"], dtype=np.float32).reshape(1, NH)),
            norm_w=np.ascontiguousarray(np.asarray(inputs["L0_attn_norm_w"], dtype=np.float32).reshape(1, DI)),
            w_out=np.ascontiguousarray(inputs["L0_attn_W_out"], dtype=np.float32),
            ffn_w=np.ascontiguousarray(inputs["L0_ffn_weight"], dtype=np.float32),
            ffn_b=np.ascontiguousarray(np.asarray(inputs["L0_ffn_bias"], dtype=np.float32).reshape(1, DM)),
        )
        m.update(consts)
        m.update(percore_consts(core))
        maps.append(m)
    return maps


def kernel(**inputs):
    nc = build_nc(0, 6)
    maps = make_in_maps(inputs)
    res = run_bass_kernel_spmd(nc, maps, core_ids=list(range(8)))
    out = np.zeros((2, 16384, DM), np.float32)
    for core in range(8):
        b, s = core // 4, core % 4
        out[b, s * T:(s + 1) * T, :] = res.results[core]["out"]
    return out
